# version 14
# baseline (speedup 1.0000x reference)
import math
import contextlib
import numpy as np

import concourse.bass as bass
import concourse.mybir as mybir
from concourse.bass_utils import run_bass_kernel_spmd

F32 = mybir.dt.float32
BF16 = mybir.dt.bfloat16
FP8 = mybir.dt.float8e4
AF = mybir.ActivationFunctionType
ALU = mybir.AluOpType
AX = mybir.AxisListType
PM = mybir.MatmulPerfMode


class Prog:

    def __init__(self, nc, sem_names):
        self.nc = nc
        self._es = contextlib.ExitStack()
        self.sems = {n: self._es.enter_context(nc.semaphore(n)) for n in sem_names}
        self.counts = {n: 0 for n in sem_names}
        self.active = None
        self.eng = None

    def snapshot(self):
        return dict(self.counts)

    def restore(self, snap):
        self.counts = dict(snap)

    def _r(self, name):
        return self.eng if self.active == name else None

    def op(self, engname, fname, *args, sem=None, inc=1, **kw):
        e = self._r(engname)
        inst = getattr(e, fname)(*args, **kw) if e is not None else None
        if sem is not None:
            self.counts[sem] += inc
            if inst is not None:
                inst.then_inc(self.sems[sem], inc)
        return self.counts[sem] if sem is not None else None

    def dma(self, engname, out, in_, sem, **kw):
        return self.op(engname, "dma_start", out, in_, sem=sem, inc=16, **kw)

    def wait(self, engname, sem, val=None):
        e = self._r(engname)
        v = self.counts[sem] if val is None else val
        if e is not None and v > 0:
            e.wait_ge(self.sems[sem], v)
        return v

    def run_block(self, fn):
        snap = self.snapshot()
        post = [None]
        with self.nc.Block() as blk:
            def mk(name):
                def body(eng):
                    self.active, self.eng = name, eng
                    self.restore(snap)
                    fn(self)
                    post[0] = self.snapshot()
                return body
            blk.gpsimd(mk("gpsimd"))
            blk.sync(mk("sync"))
            blk.tensor(mk("tensor"))
            blk.vector(mk("vector"))
            blk.scalar(mk("scalar"))
        self.restore(post[0])


def build_nc(T=2048, D=2048, H=8192, debug=False):
    assert D == 2048
    NT = T // 128
    ND = D // 128
    NH = H // 128
    NG = max(1, NH // 16)
    GH = NH // NG
    SLAB = min(4, NT)
    NSLAB = NT // SLAB
    THALF = T // 2 if NT >= 2 else T
    MH = THALF // 128
    NHALF = T // THALF
    EPS = 1e-5

    nc = bass.Bass("TRN2", target_bir_lowering=False, debug=debug)

    x_h = nc.dram_tensor("x", [T, D], F32, kind="ExternalInput").ap()
    WQ_h = nc.dram_tensor("WQ", [2048, 2048], F32, kind="ExternalInput").ap()
    bQ_h = nc.dram_tensor("bQ", [2048], F32, kind="ExternalInput").ap()
    WK_h = nc.dram_tensor("WK", [512, 2048], F32, kind="ExternalInput").ap()
    bK_h = nc.dram_tensor("bK", [512], F32, kind="ExternalInput").ap()
    WV_h = nc.dram_tensor("WV", [512, 2048], F32, kind="ExternalInput").ap()
    bV_h = nc.dram_tensor("bV", [512], F32, kind="ExternalInput").ap()
    pj_h = nc.dram_tensor("proj", [2048, 2048], F32, kind="ExternalInput").ap()
    wgb_h = nc.dram_tensor("wgb", [2, 2048, H], FP8, kind="Internal").ap()
    wlb_h = nc.dram_tensor("wlb", [H, 2048], FP8, kind="Internal").ap()
    wg_h = nc.dram_tensor("wg", [2, 2048, H], F32, kind="ExternalInput").ap()
    wl_h = nc.dram_tensor("wl", [H, 2048], F32, kind="ExternalInput").ap()
    out_h = nc.dram_tensor("out", [T, D], F32, kind="ExternalOutput").ap()

    wkb_h = nc.dram_tensor("wkb", [512, 2048], BF16, kind="Internal").ap()
    wvb_h = nc.dram_tensor("wvb", [512, 2048], BF16, kind="Internal").ap()

    es = contextlib.ExitStack()
    E = es.enter_context

    p = Prog(nc, [
        "gd",
        "gc",
        "xa", "xb",
        "ya", "yb",
        "sa", "sb",
        "wga", "wgb",
        "wla", "wlb",
        "aca", "acb",
        "wt",
        "pe", "ve", "se",
    ])

    identb = E(nc.sbuf_tensor("identb", [128, 128], BF16))
    ones1 = E(nc.sbuf_tensor("ones1", [1, 128], BF16))
    one11 = E(nc.sbuf_tensor("one11", [1, 1], F32))
    epsc = E(nc.sbuf_tensor("epsc", [128, 1], F32))

    esA = contextlib.ExitStack()
    EA = esA.enter_context
    masks = EA(nc.sbuf_tensor("masks", [128, 4, 512], BF16))
    bvrowb = EA(nc.sbuf_tensor("bvrowb", [1, 512], BF16))
    bq4c = EA(nc.sbuf_tensor("bq4c", [128, 4], F32))
    bkc = EA(nc.sbuf_tensor("bkc", [128, 4], F32))
    wq4T = EA(nc.sbuf_tensor("wq4T", [128, ND, 512], BF16))
    wkT = EA(nc.sbuf_tensor("wkT", [128, ND, 512], BF16))
    wvT = EA(nc.sbuf_tensor("wvT", [128, ND, 512], BF16))
    proj4 = EA(nc.sbuf_tensor("proj4", [128, 4, 2048], BF16))
    qsb = EA(nc.sbuf_tensor("qsb", [128, 4, SLAB * 128], BF16))
    ksb = EA(nc.sbuf_tensor("ksb", [128, 4, T], BF16))
    vsb = EA(nc.sbuf_tensor("vsb", [128, NT, 512], BF16))
    psTA = [EA(nc.psum_tensor(f"psTA{i}", [128, 256], BF16)) for i in range(2)]

    def mk_ln_bufs(E_, tag=""):
        return dict(
            xbb=[E_(nc.sbuf_tensor(f"xbb{tag}{i}", [128, 2048], F32)) for i in range(2)],
            hb=[E_(nc.sbuf_tensor(f"hb{tag}{i}", [128, 2048], BF16)) for i in range(2)],
            stats=[E_(nc.sbuf_tensor(f"stats{tag}{i}", [128, 24], F32)) for i in range(2)],
            mvr=[E_(nc.sbuf_tensor(f"mvr{tag}{i}", [128, 2], F32)) for i in range(2)],
            sinv=[E_(nc.sbuf_tensor(f"sinv{tag}{i}", [128, 1], F32)) for i in range(2)],
            nbv=[E_(nc.sbuf_tensor(f"nbv{tag}{i}", [128, 1], F32)) for i in range(2)],
        )

    def emit_ln_load(p, L, b, load_src_rows, buf_free=None):
        i = b % 2
        lsem = "xa" if i == 0 else "xb"
        if buf_free:
            p.wait("scalar", "ve", buf_free)
        p.dma("scalar", L["xbb"][i][:, :], load_src_rows, sem=lsem)
        return p.counts[lsem]

    def emit_ln_stats(p, L, b, ld_cnt):
        i = b % 2
        lsem = "xa" if i == 0 else "xb"
        p.wait("vector", lsem, ld_cnt)
        for c in range(4):
            p.op("vector", "bn_stats", L["stats"][i][:, 6 * c:6 * c + 6],
                 L["xbb"][i][:, 512 * c:512 * (c + 1)], sem="ve")
        p.wait("vector", "ve")
        p.op("vector", "bn_aggr", L["mvr"][i][:, :], L["stats"][i][:, :], sem="ve")
        p.wait("scalar", "ve")
        p.op("scalar", "activation", L["sinv"][i][:, :], L["mvr"][i][:, 1:2], AF.Sqrt,
             bias=epsc[:, :], scale=1.0, sem="se")
        p.wait("vector", "se")
        p.op("vector", "reciprocal", L["sinv"][i][:, :], L["sinv"][i][:, :], sem="ve")
        p.wait("vector", "ve")
        p.op("vector", "tensor_scalar", L["nbv"][i][:, :], L["mvr"][i][:, 0:1],
             L["sinv"][i][:, :], -1.0, op0=ALU.mult, op1=ALU.mult, sem="ve")
        return p.counts["ve"]

    def emit_ln_norm(p, L, b, hb_free=None):
        i = b % 2
        p.wait("vector", "ve")
        if hb_free is not None:
            p.wait("vector", hb_free[0], hb_free[1])
        p.op("vector", "tensor_scalar", L["hb"][i][:, :], L["xbb"][i][:, :],
             L["sinv"][i][:, :], L["nbv"][i][:, :], op0=ALU.mult, op1=ALU.add,
             sem="ve")
        return p.counts["ve"]

    def emit_transposes(p, psT, src, dstT, col0, se_ready, evac):
        p.wait("tensor", "ve", se_ready)
        for kp in range(ND // 2):
            tb = kp % 2
            if evac.get(tb):
                p.wait("tensor", evac[tb][0], evac[tb][1])
            for h in range(2):
                k = 2 * kp + h
                p.op("tensor", "matmul", psT[tb][:, 128 * h:128 * (h + 1)],
                     src[:, 128 * k:128 * (k + 1)], identb[:, :],
                     is_transpose=True, start=(h == 0), stop=(h == 1),
                     sem="pe" if h == 1 else None)
            pe_t = p.counts["pe"]
            p.wait("vector", "pe", pe_t)
            p.op("vector", "tensor_copy",
                 dstT[:, 2 * kp:2 * kp + 2, col0:col0 + 128], psT[tb][:, :], sem="ve")
            evac[tb] = ("ve", p.counts["ve"])
        return p.counts["pe"]

    esB0 = contextlib.ExitStack()
    E0 = esB0.enter_context
    colps = E0(nc.psum_tensor("colps", [128, 16], F32))
    iotac = E0(nc.sbuf_tensor("iotac", [128, 512], F32))
    iotar = E0(nc.sbuf_tensor("iotar", [128, 1], F32))
    bqrow = E0(nc.sbuf_tensor("bqrow", [1, 2048], F32))
    bq4r = E0(nc.sbuf_tensor("bq4r", [1, 512], F32))
    bkrow = E0(nc.sbuf_tensor("bkrow", [1, 512], F32))
    bvrow = E0(nc.sbuf_tensor("bvrow", [1, 512], F32))
    diffsc = E0(nc.sbuf_tensor("diffsc", [128, 512], F32))

    def b0(p):
        p.dma("gpsimd", bqrow[0:1, :], bQ_h[None, :], sem="gd")
        p.dma("gpsimd", bkrow[0:1, :], bK_h[None, :], sem="gd")
        p.dma("gpsimd", bvrow[0:1, :], bV_h[None, :], sem="gd")
        p.op("gpsimd", "iota", iotac[:, :], [[1, 512]], channel_multiplier=0,
             allow_small_or_imprecise_dtypes=True, sem="gc")
        p.op("gpsimd", "iota", iotar[:, :], [[0, 1]], channel_multiplier=1,
             allow_small_or_imprecise_dtypes=True, sem="gc")
        gd_all = p.counts["gd"]

        p.wait("vector", "gc")
        p.wait("vector", "gd", gd_all)
        p.op("vector", "tensor_scalar", diffsc[:, :], iotac[:, :], iotar[:, :],
             None, op0=ALU.subtract, sem="ve")
        p.wait("vector", "ve")
        for o in range(4):
            p.op("vector", "tensor_scalar", masks[:, o, :], diffsc[:, :],
                 float(128 * o), -1e30, op0=ALU.is_gt, op1=ALU.mult, sem="ve")
        p.wait("vector", "ve")
        p.op("vector", "tensor_scalar", identb[:, :], iotac[:, 0:128], iotar[:, :],
             None, op0=ALU.is_equal, sem="ve")
        p.op("vector", "tensor_add", bq4r[0:1, :], bqrow[0:1, 0:512], bqrow[0:1, 512:1024], sem="ve")
        p.wait("vector", "ve")
        p.op("vector", "tensor_add", bq4r[0:1, :], bq4r[0:1, :], bqrow[0:1, 1024:1536], sem="ve")
        p.wait("vector", "ve")
        p.op("vector", "tensor_add", bq4r[0:1, :], bq4r[0:1, :], bqrow[0:1, 1536:2048], sem="ve")
        p.op("vector", "memset", ones1[0:1, :], 1.0, sem="ve")
        p.op("vector", "memset", epsc[:, :], EPS, sem="ve")
        p.op("vector", "memset", one11[0:1, :], 1.0, sem="ve")
        ve_b0 = p.counts["ve"]

        p.wait("scalar", "gd", gd_all)
        p.op("scalar", "copy", bvrowb[0:1, :], bvrow[0:1, :], sem="se")

        p.wait("tensor", "ve", ve_b0)
        for i, (row, col) in enumerate([(bq4r, bq4c), (bkrow, bkc)]):
            for jt in range(4):
                p.op("tensor", "matmul", colps[:, 4 * i + jt:4 * i + jt + 1],
                     row[0:1, 128 * jt:128 * (jt + 1)], one11[0:1, 0:1],
                     start=True, stop=True, sem="pe")
                p.wait("scalar", "pe")
                p.op("scalar", "copy", col[:, jt:jt + 1],
                     colps[:, 4 * i + jt:4 * i + jt + 1], sem="se")
                p.wait("tensor", "se")

    p.run_block(b0)
    esB0.close()

    esB1 = contextlib.ExitStack()
    EB = esB1.enter_context
    LN1 = mk_ln_bufs(EB)
    h1T = EB(nc.sbuf_tensor("h1T", [128, ND, SLAB * 128], BF16))
    psb = [EB(nc.sbuf_tensor(f"psb{i}", [128, T], BF16)) for i in range(2)]
    ptsb = EB(nc.sbuf_tensor("ptsb", [128, NT, 128], BF16))
    zfb = EB(nc.sbuf_tensor("zfb", [128, 512], BF16))
    zftsb = EB(nc.sbuf_tensor("zftsb", [128, 4, 128], BF16))
    x2b = [EB(nc.sbuf_tensor(f"x2b{i}", [128, 2048], F32)) for i in range(2)]
    x1b = [EB(nc.sbuf_tensor(f"x1b{i}", [128, 2048], F32)) for i in range(2)]
    negmx = [EB(nc.sbuf_tensor(f"negmx{i}", [128, 1], F32)) for i in range(2)]
    den = [EB(nc.sbuf_tensor(f"den{i}", [128, 1], F32)) for i in range(2)]
    zsc = [EB(nc.sbuf_tensor(f"zsc{i}", [128, 1], F32)) for i in range(2)]
    wqsl = [EB(nc.sbuf_tensor(f"wqsl{r}", [128, 2048], BF16)) for r in range(3)]
    if T >= 2048:
        wqsl.append(None)
    else:
        wqsl.append(EB(nc.sbuf_tensor("wqsl3", [128, 2048], BF16)))
    psA = EB(nc.psum_tensor("psA", [128, 2048], F32))
    psZ = EB(nc.psum_tensor("psZ", [128, 512], F32))
    psP = EB(nc.psum_tensor("psP", [128, 512], F32))

    cast_cnt = {}

    if wqsl[3] is None:
        wqsl[3] = psb[0]

    def b1(p):
        p.dma("gpsimd", wkb_h[:, :], WK_h[:, :], sem="gd")
        p.dma("gpsimd", wvb_h[:, :], WV_h[:, :], sem="gd")
        gd_kv = p.counts["gd"]
        p.wait("gpsimd", "gd", gd_kv)
        p.wait("sync", "gd", gd_kv)
        for w_hbm, wT in ((wkb_h, wkT), (wvb_h, wvT)):
            for k in range(ND):
                p.dma("sync", wT[:, k, :], w_hbm[0:512, 128 * k:128 * (k + 1)],
                      sem="wt", transpose=True)
        ya_w = p.counts["wt"]
        se_ev = {}
        chunk_free = {}
        pe_jt = {}
        for jt in range(4):
            if jt > 0:
                p.wait("gpsimd", "gd")
                p.wait("gpsimd", "pe", pe_jt[jt - 1])
            for r in range(4):
                rows = slice(512 * r + 128 * jt, 512 * r + 128 * (jt + 1))
                p.dma("gpsimd", wqsl[r][:, :], WQ_h[rows, :], sem="gd")
            p.wait("tensor", "gd")
            for k in range(ND):
                c = k % 4
                if c in chunk_free:
                    p.wait("tensor", "se", chunk_free[c])
                for r in range(4):
                    p.op("tensor", "matmul", psA[:, 512 * c:512 * c + 128],
                         wqsl[r][:, 128 * k:128 * (k + 1)], identb[:, :],
                         start=(r == 0), stop=(r == 3),
                         sem="pe" if r == 3 else None)
                p.wait("scalar", "pe")
                p.op("scalar", "copy", wq4T[:, k, 128 * jt:128 * (jt + 1)],
                     psA[:, 512 * c:512 * c + 128], sem="se")
                chunk_free[c] = p.counts["se"]
            pe_jt[jt] = p.counts["pe"]
        for jt in range(4):
            p.wait("gpsimd", "gd")
            p.wait("gpsimd", "pe", p.counts["pe"])
            for r in range(4):
                rows = slice(512 * r + 128 * jt, 512 * r + 128 * (jt + 1))
                p.dma("gpsimd", wqsl[r][:, :], pj_h[rows, :], sem="gd")
            p.wait("tensor", "gd")
            for nch_ in range(4):
                if nch_ in chunk_free:
                    p.wait("tensor", "se", chunk_free[nch_])
                for r in range(4):
                    p.op("tensor", "matmul", psA[:, 512 * nch_:512 * (nch_ + 1)],
                         identb[:, :], wqsl[r][:, 512 * nch_:512 * (nch_ + 1)],
                         start=(r == 0), stop=(r == 3),
                         sem="pe" if r == 3 else None)
                p.wait("scalar", "pe")
                p.op("scalar", "copy", proj4[:, jt, 512 * nch_:512 * (nch_ + 1)],
                     psA[:, 512 * nch_:512 * (nch_ + 1)], sem="se")
                chunk_free[nch_] = p.counts["se"]
        se_prep = p.counts["se"]
        p.wait("gpsimd", "gd")
        for g in range(NG):
            if g > 0:
                p.wait("gpsimd", "gd", cast_cnt[g - 1])
            gw = GH * 128
            p.dma("gpsimd", wgb_h[0, :, gw * g:gw * (g + 1)],
                  wg_h[0, :, gw * g:gw * (g + 1)], sem="gd")
            p.dma("gpsimd", wgb_h[1, :, gw * g:gw * (g + 1)],
                  wg_h[1, :, gw * g:gw * (g + 1)], sem="gd")
            p.dma("gpsimd", wlb_h[gw * g:gw * (g + 1), :],
                  wl_h[gw * g:gw * (g + 1), :], sem="gd")
            cast_cnt[g] = p.counts["gd"]
        evac = {}
        se_h1 = {}
        x1_done = {}
        st_done = {}
        pe_pt_end = {}
        se_evac_psA = 0

        se_qkv = {}
        pe_tr = {}
        ve_norm = {}

        ld_cnt = {}

        def emit_load(b):
            ld_cnt[b] = emit_ln_load(p, LN1, b, x_h[128 * b:128 * (b + 1), :],
                                     buf_free=ve_norm.get(b - 2))

        def emit_stats(b):
            emit_ln_stats(p, LN1, b, ld_cnt[b])

        def emit_norm(b):
            hb_free = ("pe", pe_tr[b - 2]) if b - 2 in pe_tr else None
            ve_norm[b] = emit_ln_norm(p, LN1, b, hb_free=hb_free)

        def emit_trans(b):
            pe_tr[b] = emit_transposes(p, psTA, LN1["hb"][b % 2], h1T,
                                       128 * (b % SLAB), ve_norm[b], evac)

        def emit_qkv(g, psA_free):
            tcols = slice(SLAB * 128 * g, SLAB * 128 * (g + 1))
            p.wait("tensor", "ve")
            if g == 0:
                p.wait("tensor", "wt", ya_w)
                p.wait("tensor", "se", se_prep)
            if psA_free:
                p.wait("tensor", psA_free[0], psA_free[1])
            for wT, dst, bias in ((wq4T, qsb, bq4c), (wkT, ksb, bkc)):
                for jt in range(4):
                    for k in range(ND):
                        p.op("tensor", "matmul", psA[:, 512 * jt:512 * jt + SLAB * 128],
                             wT[:, k, 128 * jt:128 * (jt + 1)], h1T[:, k, 0:SLAB * 128],
                             start=(k == 0), stop=(k == ND - 1),
                             sem="pe" if k == ND - 1 else None)
                    pe_c = p.counts["pe"]
                    p.wait("scalar", "pe", pe_c)
                    dcols = slice(0, SLAB * 128) if dst is qsb else tcols
                    p.op("scalar", "activation", dst[:, jt, dcols],
                         psA[:, 512 * jt:512 * jt + SLAB * 128], AF.Identity,
                         bias=bias[:, jt:jt + 1], scale=1.0, sem="se")
                p.wait("tensor", "se")
            for st in range(SLAB):
                for k in range(ND):
                    p.op("tensor", "matmul", psA[:, 512 * st:512 * (st + 1)],
                         h1T[:, k, 128 * st:128 * (st + 1)], wvT[:, k, :],
                         start=(k == 0), stop=False)
                p.op("tensor", "matmul", psA[:, 512 * st:512 * (st + 1)],
                     ones1[0:1, :], bvrowb[0:1, :], start=False, stop=True, sem="pe")
                pe_v = p.counts["pe"]
                p.wait("scalar", "pe", pe_v)
                p.op("scalar", "activation", vsb[:, SLAB * g + st, :],
                     psA[:, 512 * st:512 * (st + 1)], AF.Identity, sem="se")
            se_qkv[g] = p.counts["se"]
            p.wait("tensor", "se", se_qkv[g])

        se_exp = {}
        ve_zlast = {}
        ve_plast = {}

        ve_zs = {}

        def emit_scores_pe(b):
            nch = b // 4 + 1
            i = b % 2
            ysem = "ya" if i == 0 else "yb"
            if b - 2 in x1_done:
                p.wait("sync", "ve", x1_done[b - 2])
            p.dma("sync", x2b[i][:, :], x_h[128 * b:128 * (b + 1), :], sem=ysem)

            if b > 0:
                p.wait("tensor", "se", se_exp[b - 1])
            p.wait("tensor", "se", se_qkv[b // SLAB])
            for c in range(nch):
                for jt in range(4):
                    p.op("tensor", "matmul", psA[:, 512 * c:512 * (c + 1)],
                         qsb[:, jt, 128 * (b % SLAB):128 * (b % SLAB) + 128],
                         ksb[:, jt, 512 * c:512 * (c + 1)],
                         start=(jt == 0), stop=(jt == 3),
                         sem="pe" if jt == 3 else None)
            return p.counts["pe"]

        def emit_softmax(b, pe_sc):
            g = b // 4
            nch = g + 1
            i = b % 2
            p.wait("vector", "pe", pe_sc)
            p.op("vector", "tensor_add", psA[:, 512 * g:512 * (g + 1)],
                 psA[:, 512 * g:512 * (g + 1)], masks[:, b % 4, :], sem="ve")
            p.wait("vector", "ve")
            p.op("vector", "reduce_max", negmx[i][:, :], psA[:, 0:512 * nch],
                 axis=AX.X, negate=True, sem="ve")
            ve_mx = p.counts["ve"]
            p.wait("scalar", "ve", ve_mx)
            if b - 2 in pe_pt_end:
                p.wait("scalar", "pe", pe_pt_end[b - 2])
            p.op("scalar", "activation", psb[i][:, 0:512 * nch], psA[:, 0:512 * nch],
                 AF.Exp, bias=negmx[i][:, :], scale=1.0, accum_out=den[i][:, :], sem="se")
            se_exp[b] = p.counts["se"]
            p.wait("vector", "se", se_exp[b])
            p.op("vector", "reciprocal", zsc[i][:, :], den[i][:, :], sem="ve")
            p.wait("vector", "ve")
            p.op("vector", "tensor_scalar_mul", zsc[i][:, :], zsc[i][:, :],
                 1.0 / math.sqrt(128.0), sem="ve")
            ve_zs[b] = p.counts["ve"]

        def emit_tail(b):
            nst = b + 1
            i = b % 2
            ysem = "ya" if i == 0 else "yb"
            p.wait("tensor", "se", se_exp[b])
            for sp in range(0, nst, 2):
                w = min(2, nst - sp)
                tb = (sp // 2) % 2
                if evac.get(tb):
                    p.wait("tensor", evac[tb][0], evac[tb][1])
                for h in range(w):
                    st = sp + h
                    p.op("tensor", "matmul", psTA[tb][:, 128 * h:128 * (h + 1)],
                         psb[i][:, 128 * st:128 * (st + 1)], identb[:, :],
                         is_transpose=True, start=(h == 0), stop=(h == w - 1),
                         sem="pe" if h == w - 1 else None)
                pe_t = p.counts["pe"]
                p.wait("scalar", "pe", pe_t)
                p.op("scalar", "copy", ptsb[:, sp:sp + w, :],
                     psTA[tb][:, 0:128 * w], sem="se")
                evac[tb] = ("se", p.counts["se"])
            pe_pt_end[b] = p.counts["pe"]
            se_pt = p.counts["se"]

            p.wait("tensor", "se", se_pt)
            if b - 1 in ve_zlast:
                p.wait("tensor", "ve", ve_zlast[b - 1])
            for st in range(nst):
                p.op("tensor", "matmul", psZ[:, :], ptsb[:, st, :], vsb[:, st, :],
                     start=(st == 0), stop=(st == nst - 1),
                     sem="pe" if st == nst - 1 else None)
            pe_z = p.counts["pe"]
            p.wait("scalar", "pe", pe_z)
            p.wait("scalar", "ve", ve_zs[b])
            p.op("scalar", "activation", zfb[:, :], psZ[:, :], AF.Identity,
                 scale=zsc[i][:, :], sem="se")
            se_zf = p.counts["se"]

            p.wait("tensor", "se", se_zf)
            for jp in range(2):
                tb = jp % 2
                if evac.get(tb):
                    p.wait("tensor", evac[tb][0], evac[tb][1])
                for h in range(2):
                    jt = 2 * jp + h
                    p.op("tensor", "matmul", psTA[tb][:, 128 * h:128 * (h + 1)],
                         zfb[:, 128 * jt:128 * (jt + 1)], identb[:, :],
                         is_transpose=True, start=(h == 0), stop=(h == 1),
                         sem="pe" if h == 1 else None)
                pe_t = p.counts["pe"]
                p.wait("scalar", "pe", pe_t)
                p.op("scalar", "copy", zftsb[:, 2 * jp:2 * jp + 2, :],
                     psTA[tb][:, :], sem="se")
                evac[tb] = ("se", p.counts["se"])
            ve_zt = p.counts["se"]

            p.wait("tensor", "se", ve_zt)
            if b - 2 in st_done:
                p.wait("vector", st_done[b - 2][0], st_done[b - 2][1])
            p.wait("vector", ysem)
            if b - 1 in ve_plast:
                p.wait("tensor", "ve", ve_plast[b - 1])
            for e in range(4):
                dst = psP if e % 2 == 0 else psZ
                if e >= 2:
                    p.wait("tensor", "ve", ve_x1c[e - 2])
                for jt in range(4):
                    p.op("tensor", "matmul", dst[:, :],
                         zftsb[:, jt, :], proj4[:, jt, 512 * e:512 * (e + 1)],
                         start=(jt == 0), stop=(jt == 3),
                         sem="pe" if jt == 3 else None)
                pe_e = p.counts["pe"]
                p.wait("vector", "pe", pe_e)
                p.op("vector", "tensor_add", x1b[i][:, 512 * e:512 * (e + 1)],
                     x2b[i][:, 512 * e:512 * (e + 1)], dst[:, :], sem="ve")
                ve_x1c[e] = p.counts["ve"]
            ve_zlast[b] = ve_x1c[3]
            ve_plast[b] = ve_x1c[2]
            x1_done[b] = ve_x1c[3]
            p.wait("sync", "ve", x1_done[b])
            ssem = "sa" if i == 0 else "sb"
            p.dma("sync", out_h[128 * b:128 * (b + 1), :], x1b[i][:, :], sem=ssem)
            st_done[b] = (ssem, p.counts[ssem])

        ve_x1c = {}
        def ln_stage(j):
            b_tr = j + SLAB
            if 0 <= b_tr < NT:
                emit_norm(b_tr)
                emit_trans(b_tr)
            b_ld = j + SLAB + 2
            if 0 <= b_ld < NT:
                emit_load(b_ld)
            b_st = j + SLAB + 1
            if 0 <= b_st < NT:
                emit_stats(b_st)

        for j in range(-(SLAB + 2), NT):
            if j >= 0:
                pe_sc = emit_scores_pe(j)
                emit_softmax(j, pe_sc)
            ln_stage(j)
            if j >= 1:
                emit_tail(j - 1)
            if j == -1:
                emit_qkv(0, None)
            elif j >= 0 and (j + 1) % SLAB == 0 and 1 <= (j + 1) // SLAB < NSLAB:
                emit_qkv((j + 1) // SLAB, ("se", se_exp[j]))
        emit_tail(NT - 1)

        p.wait("sync", "sa")
        p.wait("sync", "sb")

    p.run_block(b1)
    esB1.close()
    esA.close()

    esH = contextlib.ExitStack()
    EH = esH.enter_context
    h2T = EH(nc.sbuf_tensor("h2T", [128, ND, THALF], FP8))
    L2 = mk_ln_bufs(EH, tag="f_")
    act = EH(nc.sbuf_tensor("act", [128, GH, THALF], FP8))
    wgq = [[EH(nc.sbuf_tensor(f"wgq{gi}_{i}", [128, ND, 512], FP8))
            for i in range(2)] for gi in range(2)]
    wlsb = [EH(nc.sbuf_tensor(f"wlsb{i}", [128, GH, 1024], FP8)) for i in range(2)]
    gel = EH(nc.sbuf_tensor("gel", [128, THALF], BF16))
    stg = [EH(nc.sbuf_tensor(f"stg{i}", [128, 1024], F32)) for i in range(2)]
    psF0 = EH(nc.psum_tensor("psF0", [128, 2048], F32))
    psF1 = EH(nc.psum_tensor("psF1", [128, 2048], F32))

    NCH = max(1, THALF // 512)
    CW = min(512, THALF)

    def ffn(p):
        f0_free = {}
        f1_free = {}
        ve_flush = {}
        ac_cnt = {}
        wg_loaded = {}
        wl_loaded = {}
        st = dict(pe_accum_end=0, prev_norm={}, prev_tr={}, prev_act={})

        def emit_trans_f32(b, ve_ready, rot):
            p.wait("tensor", "ve", ve_ready)
            for kp in range(ND // 2):
                r = (rot + kp) % 4
                if r in f1_free:
                    p.wait("tensor", f1_free[r][0], f1_free[r][1])
                for h in range(2):
                    k = 2 * kp + h
                    p.op("tensor", "matmul", psF1[:, 512 * r + 128 * h:512 * r + 128 * (h + 1)],
                         L2["hb"][b % 2][:, 128 * k:128 * (k + 1)], identb[:, :],
                         start=(h == 0), stop=(h == 1),
                         sem="pe" if h == 1 else None)
                pe_t = p.counts["pe"]
                p.wait("vector", "pe", pe_t)
                p.op("vector", "tensor_copy",
                     h2T[:, 2 * kp:2 * kp + 2, 128 * (b % MH):128 * (b % MH) + 128],
                     psF1[:, 512 * r:512 * r + 256], sem="ve")
                f1_free[r] = ("ve", p.counts["ve"])
            return p.counts["pe"]

        def b2_stages(hf):
            pe_tr2 = dict(st["prev_tr"])
            ve_n2 = dict(st["prev_norm"])
            ld2 = {}
            for j in range(-4, MH - 1):
                mm = j + 2
                if 0 <= mm < MH:
                    hbf = ("pe", pe_tr2[mm - 2]) if mm - 2 in pe_tr2 else None
                    ve_n2[mm] = emit_ln_norm(p, L2, mm, hb_free=hbf)
                    pe_tr2[mm] = emit_trans_f32(mm, ve_n2[mm], rot=2 * mm)
                ml = j + 4
                if 0 <= ml < MH:
                    tb = MH * hf + ml
                    ld2[ml] = emit_ln_load(p, L2, ml, out_h[128 * tb:128 * (tb + 1), :],
                                           buf_free=ve_n2.get(ml - 2))
                ms = j + 3
                if 0 <= ms < MH:
                    emit_ln_stats(p, L2, ms, ld2[ms])
            st["prev_tr"] = {-2: pe_tr2[MH - 2], -1: pe_tr2[MH - 1]}
            st["prev_norm"] = {-2: ve_n2[MH - 2], -1: ve_n2[MH - 1]}
            st["h2T_ready"] = p.counts["ve"]

        def load_wg_quarter(qi, buf, first=False):
            wsem = "wga" if buf == 0 else "wgb"
            col = 512 * qi
            if first:
                p.wait("sync", "pe", p.counts["pe"])
            p.wait("sync", "gd", cast_cnt[qi * 512 // (GH * 128)])
            for gi in range(2):
                for kk in range(ND // 4):
                    src = wgb_h[gi, 512 * kk:512 * (kk + 1), col:col + 512]
                    src = src.rearrange("(k p) c -> p k c", p=128)
                    p.dma("sync", wgq[gi][buf][:, 4 * kk:4 * (kk + 1), :],
                          src, sem=wsem)
            wg_loaded[buf] = p.counts[wsem]
            return wsem

        def load_wl(g, eh, buf):
            wsem = "wla" if buf == 0 else "wlb"
            p.wait("scalar", "gd", cast_cnt[g])
            if st["pe_accum_end"]:
                p.wait("scalar", "pe", st["pe_accum_end"])
            for kk in range(GH // 4):
                r = 128 * GH * g + 512 * kk
                src = wlb_h[r:r + 512, 1024 * eh:1024 * (eh + 1)]
                src = src.rearrange("(k p) c -> p k c", p=128)
                p.dma("scalar", wlsb[buf][:, 4 * kk:4 * (kk + 1), :],
                      src, sem=wsem)
            wl_loaded[buf] = p.counts[wsem]
            return wsem

        def b3_stages(hf):
            ve_act = dict(st["prev_act"])
            se_gel = {}
            p.wait("tensor", "ve", st["h2T_ready"])
            load_wg_quarter(0, 0, first=True)
            wl_sems = {}
            for g in range(NG):
                wl_sems[0] = load_wl(g, 0, 0)
                wl_sems[1] = load_wl(g, 1, 1)
                nq = max(1, GH // 4)
                hq_per = min(4, GH)
                for q in range(nq):
                    buf = q % 2
                    qi = nq * g + q
                    wsem = "wga" if buf == 0 else "wgb"
                    p.wait("tensor", wsem, wg_loaded[buf])
                    if qi + 1 < nq * NG:
                        p.wait("sync", "pe", p.counts["pe"])
                        load_wg_quarter(qi + 1, 1 - buf)
                    for hq in range(hq_per):
                        gh = hq_per * q + hq
                        par = gh % 2
                        for n in range(NCH):
                            r = 2 * par + n
                            nsl = slice(CW * n, CW * (n + 1))
                            fsl = slice(512 * r, 512 * r + CW)
                            if r in f0_free:
                                p.wait("tensor", f0_free[r][0], f0_free[r][1])
                            ND2 = ND // 2
                            for k in range(ND2):
                                p.op("tensor", "matmul", psF0[:, fsl],
                                     wgq[0][buf][:, 2 * k:2 * k + 2, 128 * hq:128 * (hq + 1)],
                                     h2T[:, 2 * k:2 * k + 2, nsl],
                                     start=(k == 0), stop=(k == ND2 - 1),
                                     perf_mode=PM.DoubleRow,
                                     sem="pe" if k == ND2 - 1 else None)
                            pe_g0 = p.counts["pe"]
                            if r in f1_free:
                                p.wait("tensor", f1_free[r][0], f1_free[r][1])
                            for k in range(ND2):
                                p.op("tensor", "matmul", psF1[:, fsl],
                                     wgq[1][buf][:, 2 * k:2 * k + 2, 128 * hq:128 * (hq + 1)],
                                     h2T[:, 2 * k:2 * k + 2, nsl],
                                     start=(k == 0), stop=(k == ND2 - 1),
                                     perf_mode=PM.DoubleRow,
                                     sem="pe" if k == ND2 - 1 else None)
                            pe_g1 = p.counts["pe"]
                            p.wait("scalar", "pe", pe_g0)
                            if (gh - 1, n) in ve_act:
                                p.wait("scalar", "ve", ve_act[(gh - 1, n)])
                            p.op("scalar", "activation", gel[:, nsl], psF0[:, fsl],
                                 AF.Gelu_apprx_tanh, sem="se")
                            se_gel[(gh, n)] = p.counts["se"]
                            f0_free[r] = ("se", se_gel[(gh, n)])
                            p.wait("vector", "pe", pe_g1)
                            p.wait("vector", "se", se_gel[(gh, n)])
                            if st["pe_accum_end"]:
                                p.wait("vector", "pe", st["pe_accum_end"])
                            p.op("vector", "tensor_mul", act[:, gh, nsl],
                                 gel[:, nsl], psF1[:, fsl], sem="ve")
                            ve_act[(gh, n)] = p.counts["ve"]
                            f1_free[r] = ("ve", ve_act[(gh, n)])
                ve_act_all = p.counts["ve"]

                p.wait("tensor", "ve", ve_act_all)
                for eh in range(2):
                    p.wait("tensor", wl_sems[eh], wl_loaded[eh])
                    for m in range(MH):
                        ob = m % 2
                        psO = psF0[:, 1024 * ob:1024 * (ob + 1)]
                        for n in range(2):
                            r = 2 * ob + n
                            if r in f0_free:
                                p.wait("tensor", f0_free[r][0], f0_free[r][1])
                        GH2 = GH // 2
                        for h in range(GH2):
                            for e2 in range(2):
                                p.op("tensor", "matmul", psO[:, 512 * e2:512 * (e2 + 1)],
                                     act[:, 2 * h:2 * h + 2, 128 * m:128 * (m + 1)],
                                     wlsb[eh][:, 2 * h:2 * h + 2, 512 * e2:512 * (e2 + 1)],
                                     start=(h == 0), stop=(h == GH2 - 1),
                                     perf_mode=PM.DoubleRow,
                                     sem="pe" if (h == GH2 - 1 and e2 == 1) else None)
                        pe_o = p.counts["pe"]
                        p.wait("vector", "pe", pe_o)
                        asem = "aca" if ob == 0 else "acb"
                        if ob in ac_cnt:
                            p.wait("vector", asem, ac_cnt[ob])
                        p.op("vector", "tensor_copy", stg[ob][:, :], psO[:, :], sem="ve")
                        ve_flush[ob] = p.counts["ve"]
                        for n in range(2):
                            f0_free[2 * ob + n] = ("ve", ve_flush[ob])
                        p.wait("gpsimd", "ve", ve_flush[ob])
                        r0 = 128 * (MH * hf + m)
                        p.dma("gpsimd", out_h[r0:r0 + 128, 1024 * eh:1024 * (eh + 1)],
                              stg[ob][:, :], sem=asem, accum_op=ALU.add)
                        ac_cnt[ob] = p.counts[asem]
                st["pe_accum_end"] = p.counts["pe"]
                st["prev_act"] = {(-1, n): ve_act_all for n in range(NCH)}
                ve_act = dict(st["prev_act"])
                se_gel = {}

        for hf in range(NHALF):
            b2_stages(hf)
            b3_stages(hf)
        p.wait("gpsimd", "aca")
        p.wait("gpsimd", "acb")

    p.run_block(ffn)
    esH.close()
    es.close()
    p._es.close()
    return nc


def kernel(L0_attn_x, L0_attn_WQ, L0_attn_bQ, L0_attn_WK, L0_attn_bK,
           L0_attn_WV, L0_attn_bV, L0_attn_proj, L0_ffn_w_gating, L0_ffn_w_linear):
    x = np.ascontiguousarray(np.asarray(L0_attn_x, dtype=np.float32))
    W = dict(
        WQ=np.ascontiguousarray(np.asarray(L0_attn_WQ, dtype=np.float32)),
        bQ=np.ascontiguousarray(np.asarray(L0_attn_bQ, dtype=np.float32)),
        WK=np.ascontiguousarray(np.asarray(L0_attn_WK, dtype=np.float32)),
        bK=np.ascontiguousarray(np.asarray(L0_attn_bK, dtype=np.float32)),
        WV=np.ascontiguousarray(np.asarray(L0_attn_WV, dtype=np.float32)),
        bV=np.ascontiguousarray(np.asarray(L0_attn_bV, dtype=np.float32)),
        proj=np.ascontiguousarray(np.asarray(L0_attn_proj, dtype=np.float32)),
        wg=np.ascontiguousarray(np.asarray(L0_ffn_w_gating, dtype=np.float32)),
        wl=np.ascontiguousarray(np.asarray(L0_ffn_w_linear, dtype=np.float32)),
    )
    nc = build_nc(T=2048, D=2048, H=8192)
    in_maps = [dict(x=x[c], **W) for c in range(8)]
    res = run_bass_kernel_spmd(nc, in_maps, core_ids=list(range(8)))
    return np.stack([np.asarray(r["out"], dtype=np.float32) for r in res.results], axis=0)


# version 15
# speedup vs baseline: 1.1068x; 1.1068x over previous
import math
import contextlib
import numpy as np

import concourse.bass as bass
import concourse.mybir as mybir
from concourse.bass_utils import run_bass_kernel_spmd

F32 = mybir.dt.float32
BF16 = mybir.dt.bfloat16
FP8 = mybir.dt.float8e4
AF = mybir.ActivationFunctionType
ALU = mybir.AluOpType
AX = mybir.AxisListType
PM = mybir.MatmulPerfMode


class Prog:

    def __init__(self, nc, sem_names):
        self.nc = nc
        self._es = contextlib.ExitStack()
        self.sems = {n: self._es.enter_context(nc.semaphore(n)) for n in sem_names}
        self.counts = {n: 0 for n in sem_names}
        self.active = None
        self.eng = None

    def snapshot(self):
        return dict(self.counts)

    def restore(self, snap):
        self.counts = dict(snap)

    def _r(self, name):
        return self.eng if self.active == name else None

    def op(self, engname, fname, *args, sem=None, inc=1, **kw):
        e = self._r(engname)
        inst = getattr(e, fname)(*args, **kw) if e is not None else None
        if sem is not None:
            self.counts[sem] += inc
            if inst is not None:
                inst.then_inc(self.sems[sem], inc)
        return self.counts[sem] if sem is not None else None

    def dma(self, engname, out, in_, sem, **kw):
        return self.op(engname, "dma_start", out, in_, sem=sem, inc=16, **kw)

    def wait(self, engname, sem, val=None):
        e = self._r(engname)
        v = self.counts[sem] if val is None else val
        if e is not None and v > 0:
            e.wait_ge(self.sems[sem], v)
        return v

    def run_block(self, fn):
        snap = self.snapshot()
        post = [None]
        with self.nc.Block() as blk:
            def mk(name):
                def body(eng):
                    self.active, self.eng = name, eng
                    self.restore(snap)
                    fn(self)
                    post[0] = self.snapshot()
                return body
            blk.gpsimd(mk("gpsimd"))
            blk.sync(mk("sync"))
            blk.tensor(mk("tensor"))
            blk.vector(mk("vector"))
            blk.scalar(mk("scalar"))
        self.restore(post[0])


def build_nc(T=2048, D=2048, H=8192, debug=False):
    assert D == 2048
    NT = T // 128
    ND = D // 128
    NH = H // 128
    NG = max(1, NH // 16)
    GH = NH // NG
    SLAB = min(4, NT)
    NSLAB = NT // SLAB
    THALF = T // 2 if NT >= 2 else T
    MH = THALF // 128
    NHALF = T // THALF
    EPS = 1e-5

    nc = bass.Bass("TRN2", target_bir_lowering=False, debug=debug)

    x_h = nc.dram_tensor("x", [T, D], F32, kind="ExternalInput").ap()
    WQ_h = nc.dram_tensor("WQ", [2048, 2048], F32, kind="ExternalInput").ap()
    bQ_h = nc.dram_tensor("bQ", [2048], F32, kind="ExternalInput").ap()
    WK_h = nc.dram_tensor("WK", [512, 2048], F32, kind="ExternalInput").ap()
    bK_h = nc.dram_tensor("bK", [512], F32, kind="ExternalInput").ap()
    WV_h = nc.dram_tensor("WV", [512, 2048], F32, kind="ExternalInput").ap()
    bV_h = nc.dram_tensor("bV", [512], F32, kind="ExternalInput").ap()
    pj_h = nc.dram_tensor("proj", [2048, 2048], F32, kind="ExternalInput").ap()
    wgb_h = nc.dram_tensor("wgb", [2, 2048, H], FP8, kind="Internal").ap()
    wlb_h = nc.dram_tensor("wlb", [H, 2048], FP8, kind="Internal").ap()
    wg_h = nc.dram_tensor("wg", [2, 2048, H], F32, kind="ExternalInput").ap()
    wl_h = nc.dram_tensor("wl", [H, 2048], F32, kind="ExternalInput").ap()
    out_h = nc.dram_tensor("out", [T, D], F32, kind="ExternalOutput").ap()

    wkb_h = nc.dram_tensor("wkb", [512, 2048], BF16, kind="Internal").ap()
    wvb_h = nc.dram_tensor("wvb", [512, 2048], BF16, kind="Internal").ap()

    es = contextlib.ExitStack()
    E = es.enter_context

    p = Prog(nc, [
        "gd",
        "gc",
        "xa", "xb",
        "ya", "yb",
        "sa", "sb",
        "wga", "wgb",
        "wla", "wlb",
        "aca", "acb",
        "wt",
        "pe", "ve", "se",
    ])

    identb = E(nc.sbuf_tensor("identb", [128, 128], BF16))
    ones1 = E(nc.sbuf_tensor("ones1", [1, 128], BF16))
    one11 = E(nc.sbuf_tensor("one11", [1, 1], F32))
    epsc = E(nc.sbuf_tensor("epsc", [128, 1], F32))

    esA = contextlib.ExitStack()
    EA = esA.enter_context
    masks = EA(nc.sbuf_tensor("masks", [128, 4, 512], BF16))
    bvrowb = EA(nc.sbuf_tensor("bvrowb", [1, 512], BF16))
    bq4c = EA(nc.sbuf_tensor("bq4c", [128, 4], F32))
    bkc = EA(nc.sbuf_tensor("bkc", [128, 4], F32))
    wq4T = EA(nc.sbuf_tensor("wq4T", [128, ND, 512], BF16))
    wkT = EA(nc.sbuf_tensor("wkT", [128, ND, 512], BF16))
    wvT = EA(nc.sbuf_tensor("wvT", [128, ND, 512], BF16))
    proj4 = EA(nc.sbuf_tensor("proj4", [128, 4, 2048], BF16))
    qsb = EA(nc.sbuf_tensor("qsb", [128, 4, SLAB * 128], BF16))
    ksb = EA(nc.sbuf_tensor("ksb", [128, 4, T], BF16))
    vsb = EA(nc.sbuf_tensor("vsb", [128, NT, 512], BF16))
    psTA = [EA(nc.psum_tensor(f"psTA{i}", [128, 256], BF16)) for i in range(2)]

    def mk_ln_bufs(E_, tag=""):
        return dict(
            xbb=[E_(nc.sbuf_tensor(f"xbb{tag}{i}", [128, 2048], F32)) for i in range(2)],
            hb=[E_(nc.sbuf_tensor(f"hb{tag}{i}", [128, 2048], BF16)) for i in range(2)],
            stats=[E_(nc.sbuf_tensor(f"stats{tag}{i}", [128, 24], F32)) for i in range(2)],
            mvr=[E_(nc.sbuf_tensor(f"mvr{tag}{i}", [128, 2], F32)) for i in range(2)],
            sinv=[E_(nc.sbuf_tensor(f"sinv{tag}{i}", [128, 1], F32)) for i in range(2)],
            nbv=[E_(nc.sbuf_tensor(f"nbv{tag}{i}", [128, 1], F32)) for i in range(2)],
        )

    def emit_ln_load(p, L, b, load_src_rows, buf_free=None):
        i = b % 2
        lsem = "xa" if i == 0 else "xb"
        if buf_free:
            p.wait("scalar", "ve", buf_free)
        p.dma("scalar", L["xbb"][i][:, :], load_src_rows, sem=lsem)
        return p.counts[lsem]

    def emit_ln_stats(p, L, b, ld_cnt):
        i = b % 2
        lsem = "xa" if i == 0 else "xb"
        p.wait("vector", lsem, ld_cnt)
        for c in range(4):
            p.op("vector", "bn_stats", L["stats"][i][:, 6 * c:6 * c + 6],
                 L["xbb"][i][:, 512 * c:512 * (c + 1)], sem="ve")
        p.wait("vector", "ve")
        p.op("vector", "bn_aggr", L["mvr"][i][:, :], L["stats"][i][:, :], sem="ve")
        p.wait("scalar", "ve")
        p.op("scalar", "activation", L["sinv"][i][:, :], L["mvr"][i][:, 1:2], AF.Sqrt,
             bias=epsc[:, :], scale=1.0, sem="se")
        p.wait("vector", "se")
        p.op("vector", "reciprocal", L["sinv"][i][:, :], L["sinv"][i][:, :], sem="ve")
        p.wait("vector", "ve")
        p.op("vector", "tensor_scalar", L["nbv"][i][:, :], L["mvr"][i][:, 0:1],
             L["sinv"][i][:, :], -1.0, op0=ALU.mult, op1=ALU.mult, sem="ve")
        return p.counts["ve"]

    def emit_ln_norm(p, L, b, hb_free=None):
        i = b % 2
        p.wait("vector", "ve")
        if hb_free is not None:
            p.wait("vector", hb_free[0], hb_free[1])
        p.op("vector", "tensor_scalar", L["hb"][i][:, :], L["xbb"][i][:, :],
             L["sinv"][i][:, :], L["nbv"][i][:, :], op0=ALU.mult, op1=ALU.add,
             sem="ve")
        return p.counts["ve"]

    def emit_transposes(p, psT, src, dstT, col0, se_ready, evac):
        p.wait("tensor", "ve", se_ready)
        for kp in range(ND // 2):
            tb = kp % 2
            if evac.get(tb):
                p.wait("tensor", evac[tb][0], evac[tb][1])
            for h in range(2):
                k = 2 * kp + h
                p.op("tensor", "matmul", psT[tb][:, 128 * h:128 * (h + 1)],
                     src[:, 128 * k:128 * (k + 1)], identb[:, :],
                     is_transpose=True, start=(h == 0), stop=(h == 1),
                     sem="pe" if h == 1 else None)
            pe_t = p.counts["pe"]
            p.wait("vector", "pe", pe_t)
            p.op("vector", "tensor_copy",
                 dstT[:, 2 * kp:2 * kp + 2, col0:col0 + 128], psT[tb][:, :], sem="ve")
            evac[tb] = ("ve", p.counts["ve"])
        return p.counts["pe"]

    esB0 = contextlib.ExitStack()
    E0 = esB0.enter_context
    colps = E0(nc.psum_tensor("colps", [128, 16], F32))
    iotac = E0(nc.sbuf_tensor("iotac", [128, 512], F32))
    iotar = E0(nc.sbuf_tensor("iotar", [128, 1], F32))
    bqrow = E0(nc.sbuf_tensor("bqrow", [1, 2048], F32))
    bq4r = E0(nc.sbuf_tensor("bq4r", [1, 512], F32))
    bkrow = E0(nc.sbuf_tensor("bkrow", [1, 512], F32))
    bvrow = E0(nc.sbuf_tensor("bvrow", [1, 512], F32))
    diffsc = E0(nc.sbuf_tensor("diffsc", [128, 512], F32))

    def b0(p):
        p.dma("gpsimd", bqrow[0:1, :], bQ_h[None, :], sem="gd")
        p.dma("gpsimd", bkrow[0:1, :], bK_h[None, :], sem="gd")
        p.dma("gpsimd", bvrow[0:1, :], bV_h[None, :], sem="gd")
        p.op("gpsimd", "iota", iotac[:, :], [[1, 512]], channel_multiplier=0,
             allow_small_or_imprecise_dtypes=True, sem="gc")
        p.op("gpsimd", "iota", iotar[:, :], [[0, 1]], channel_multiplier=1,
             allow_small_or_imprecise_dtypes=True, sem="gc")
        gd_all = p.counts["gd"]

        p.wait("vector", "gc")
        p.wait("vector", "gd", gd_all)
        p.op("vector", "tensor_scalar", diffsc[:, :], iotac[:, :], iotar[:, :],
             None, op0=ALU.subtract, sem="ve")
        p.wait("vector", "ve")
        for o in range(4):
            p.op("vector", "tensor_scalar", masks[:, o, :], diffsc[:, :],
                 float(128 * o), -1e30, op0=ALU.is_gt, op1=ALU.mult, sem="ve")
        p.wait("vector", "ve")
        p.op("vector", "tensor_scalar", identb[:, :], iotac[:, 0:128], iotar[:, :],
             None, op0=ALU.is_equal, sem="ve")
        p.op("vector", "tensor_add", bq4r[0:1, :], bqrow[0:1, 0:512], bqrow[0:1, 512:1024], sem="ve")
        p.wait("vector", "ve")
        p.op("vector", "tensor_add", bq4r[0:1, :], bq4r[0:1, :], bqrow[0:1, 1024:1536], sem="ve")
        p.wait("vector", "ve")
        p.op("vector", "tensor_add", bq4r[0:1, :], bq4r[0:1, :], bqrow[0:1, 1536:2048], sem="ve")
        p.op("vector", "memset", ones1[0:1, :], 1.0, sem="ve")
        p.op("vector", "memset", epsc[:, :], EPS, sem="ve")
        p.op("vector", "memset", one11[0:1, :], 1.0, sem="ve")
        ve_b0 = p.counts["ve"]

        p.wait("scalar", "gd", gd_all)
        p.op("scalar", "copy", bvrowb[0:1, :], bvrow[0:1, :], sem="se")

        p.wait("tensor", "ve", ve_b0)
        for i, (row, col) in enumerate([(bq4r, bq4c), (bkrow, bkc)]):
            for jt in range(4):
                p.op("tensor", "matmul", colps[:, 4 * i + jt:4 * i + jt + 1],
                     row[0:1, 128 * jt:128 * (jt + 1)], one11[0:1, 0:1],
                     start=True, stop=True, sem="pe")
                p.wait("scalar", "pe")
                p.op("scalar", "copy", col[:, jt:jt + 1],
                     colps[:, 4 * i + jt:4 * i + jt + 1], sem="se")
                p.wait("tensor", "se")

    p.run_block(b0)
    esB0.close()

    esB1 = contextlib.ExitStack()
    EB = esB1.enter_context
    LN1 = mk_ln_bufs(EB)
    h1T = EB(nc.sbuf_tensor("h1T", [128, ND, SLAB * 128], BF16))
    psb = [EB(nc.sbuf_tensor(f"psb{i}", [128, T], BF16)) for i in range(2)]
    ptsb = EB(nc.sbuf_tensor("ptsb", [128, NT, 128], BF16))
    zfb = EB(nc.sbuf_tensor("zfb", [128, 512], BF16))
    zftsb = EB(nc.sbuf_tensor("zftsb", [128, 4, 128], BF16))
    x2b = [EB(nc.sbuf_tensor(f"x2b{i}", [128, 2048], F32)) for i in range(2)]
    x1b = [EB(nc.sbuf_tensor(f"x1b{i}", [128, 2048], F32)) for i in range(2)]
    negmx = [EB(nc.sbuf_tensor(f"negmx{i}", [128, 1], F32)) for i in range(2)]
    den = [EB(nc.sbuf_tensor(f"den{i}", [128, 1], F32)) for i in range(2)]
    zsc = [EB(nc.sbuf_tensor(f"zsc{i}", [128, 1], F32)) for i in range(2)]
    wqsl = [EB(nc.sbuf_tensor(f"wqsl{r}", [128, 2048], BF16)) for r in range(3)]
    if T >= 2048:
        wqsl.append(None)
    else:
        wqsl.append(EB(nc.sbuf_tensor("wqsl3", [128, 2048], BF16)))
    psA = EB(nc.psum_tensor("psA", [128, 2048], F32))
    psZ = EB(nc.psum_tensor("psZ", [128, 512], F32))
    psP = EB(nc.psum_tensor("psP", [128, 512], F32))

    cast_cnt = {}

    if wqsl[3] is None:
        wqsl[3] = psb[0]

    def b1(p):
        p.dma("gpsimd", wkb_h[:, :], WK_h[:, :], sem="gd")
        p.dma("gpsimd", wvb_h[:, :], WV_h[:, :], sem="gd")
        gd_kv = p.counts["gd"]
        p.wait("gpsimd", "gd", gd_kv)
        p.wait("sync", "gd", gd_kv)
        for w_hbm, wT in ((wkb_h, wkT), (wvb_h, wvT)):
            for k in range(ND):
                p.dma("sync", wT[:, k, :], w_hbm[0:512, 128 * k:128 * (k + 1)],
                      sem="wt", transpose=True)
        ya_w = p.counts["wt"]
        se_ev = {}
        chunk_free = {}
        pe_jt = {}
        for jt in range(4):
            if jt > 0:
                p.wait("gpsimd", "gd")
                p.wait("gpsimd", "pe", pe_jt[jt - 1])
            for r in range(4):
                rows = slice(512 * r + 128 * jt, 512 * r + 128 * (jt + 1))
                p.dma("gpsimd", wqsl[r][:, :], WQ_h[rows, :], sem="gd")
            p.wait("tensor", "gd")
            for k in range(ND):
                c = k % 4
                if c in chunk_free:
                    p.wait("tensor", "se", chunk_free[c])
                for r in range(4):
                    p.op("tensor", "matmul", psA[:, 512 * c:512 * c + 128],
                         wqsl[r][:, 128 * k:128 * (k + 1)], identb[:, :],
                         start=(r == 0), stop=(r == 3),
                         sem="pe" if r == 3 else None)
                p.wait("scalar", "pe")
                p.op("scalar", "copy", wq4T[:, k, 128 * jt:128 * (jt + 1)],
                     psA[:, 512 * c:512 * c + 128], sem="se")
                chunk_free[c] = p.counts["se"]
            pe_jt[jt] = p.counts["pe"]
        for jt in range(4):
            p.wait("gpsimd", "gd")
            p.wait("gpsimd", "pe", p.counts["pe"])
            for r in range(4):
                rows = slice(512 * r + 128 * jt, 512 * r + 128 * (jt + 1))
                p.dma("gpsimd", wqsl[r][:, :], pj_h[rows, :], sem="gd")
            p.wait("tensor", "gd")
            for nch_ in range(4):
                if nch_ in chunk_free:
                    p.wait("tensor", "se", chunk_free[nch_])
                for r in range(4):
                    p.op("tensor", "matmul", psA[:, 512 * nch_:512 * (nch_ + 1)],
                         identb[:, :], wqsl[r][:, 512 * nch_:512 * (nch_ + 1)],
                         start=(r == 0), stop=(r == 3),
                         sem="pe" if r == 3 else None)
                p.wait("scalar", "pe")
                p.op("scalar", "copy", proj4[:, jt, 512 * nch_:512 * (nch_ + 1)],
                     psA[:, 512 * nch_:512 * (nch_ + 1)], sem="se")
                chunk_free[nch_] = p.counts["se"]
        se_prep = p.counts["se"]
        p.wait("gpsimd", "gd")
        gw = GH * 128
        nup = min(2, NG)
        for g in range(nup):
            if g > 0:
                p.wait("gpsimd", "gd", cast_cnt[g - 1])
            p.dma("gpsimd", wgb_h[0, :, gw * g:gw * (g + 1)],
                  wg_h[0, :, gw * g:gw * (g + 1)], sem="gd")
            p.dma("gpsimd", wgb_h[1, :, gw * g:gw * (g + 1)],
                  wg_h[1, :, gw * g:gw * (g + 1)], sem="gd")
            p.dma("gpsimd", wlb_h[gw * g:gw * (g + 1), :],
                  wl_h[gw * g:gw * (g + 1), :], sem="gd")
            cast_cnt[g] = p.counts["gd"]
        p.wait("gpsimd", "gd")
        cast_chunks = []
        for g in range(nup, NG):
            for gi in range(2):
                for half_ in range(2):
                    rs = slice(1024 * half_, 1024 * (half_ + 1))
                    cast_chunks.append((g, (wgb_h[gi, rs, gw * g:gw * (g + 1)],
                                            wg_h[gi, rs, gw * g:gw * (g + 1)])))
            for half_ in range(2):
                rs = slice(gw * g + 1024 * half_, gw * g + 1024 * (half_ + 1))
                cast_chunks.append((g, (wlb_h[rs, :], wl_h[rs, :])))
        evac = {}
        se_h1 = {}
        x1_done = {}
        st_done = {}
        pe_pt_end = {}
        se_evac_psA = 0

        se_qkv = {}
        pe_tr = {}
        ve_norm = {}

        ld_cnt = {}

        def emit_load(b):
            ld_cnt[b] = emit_ln_load(p, LN1, b, x_h[128 * b:128 * (b + 1), :],
                                     buf_free=ve_norm.get(b - 2))

        def emit_stats(b):
            emit_ln_stats(p, LN1, b, ld_cnt[b])

        def emit_norm(b):
            hb_free = ("pe", pe_tr[b - 2]) if b - 2 in pe_tr else None
            ve_norm[b] = emit_ln_norm(p, LN1, b, hb_free=hb_free)

        def emit_trans(b):
            pe_tr[b] = emit_transposes(p, psTA, LN1["hb"][b % 2], h1T,
                                       128 * (b % SLAB), ve_norm[b], evac)

        def emit_qkv(g, psA_free):
            tcols = slice(SLAB * 128 * g, SLAB * 128 * (g + 1))
            p.wait("tensor", "ve")
            if g == 0:
                p.wait("tensor", "wt", ya_w)
                p.wait("tensor", "se", se_prep)
            if psA_free:
                p.wait("tensor", psA_free[0], psA_free[1])
            for wT, dst, bias in ((wq4T, qsb, bq4c), (wkT, ksb, bkc)):
                for jt in range(4):
                    for k in range(ND):
                        p.op("tensor", "matmul", psA[:, 512 * jt:512 * jt + SLAB * 128],
                             wT[:, k, 128 * jt:128 * (jt + 1)], h1T[:, k, 0:SLAB * 128],
                             start=(k == 0), stop=(k == ND - 1),
                             sem="pe" if k == ND - 1 else None)
                    pe_c = p.counts["pe"]
                    p.wait("scalar", "pe", pe_c)
                    dcols = slice(0, SLAB * 128) if dst is qsb else tcols
                    p.op("scalar", "activation", dst[:, jt, dcols],
                         psA[:, 512 * jt:512 * jt + SLAB * 128], AF.Identity,
                         bias=bias[:, jt:jt + 1], scale=1.0, sem="se")
                p.wait("tensor", "se")
            for st in range(SLAB):
                for k in range(ND):
                    p.op("tensor", "matmul", psA[:, 512 * st:512 * (st + 1)],
                         h1T[:, k, 128 * st:128 * (st + 1)], wvT[:, k, :],
                         start=(k == 0), stop=False)
                p.op("tensor", "matmul", psA[:, 512 * st:512 * (st + 1)],
                     ones1[0:1, :], bvrowb[0:1, :], start=False, stop=True, sem="pe")
                pe_v = p.counts["pe"]
                p.wait("scalar", "pe", pe_v)
                p.op("scalar", "activation", vsb[:, SLAB * g + st, :],
                     psA[:, 512 * st:512 * (st + 1)], AF.Identity, sem="se")
            se_qkv[g] = p.counts["se"]
            p.wait("tensor", "se", se_qkv[g])

        se_exp = {}
        ve_zlast = {}
        ve_plast = {}

        ve_zs = {}

        def emit_scores_pe(b):
            nch = b // 4 + 1
            i = b % 2
            ysem = "ya" if i == 0 else "yb"
            if b - 2 in x1_done:
                p.wait("sync", "ve", x1_done[b - 2])
            p.dma("sync", x2b[i][:, :], x_h[128 * b:128 * (b + 1), :], sem=ysem)

            if b > 0:
                p.wait("tensor", "se", se_exp[b - 1])
            p.wait("tensor", "se", se_qkv[b // SLAB])
            for c in range(nch):
                for jt in range(4):
                    p.op("tensor", "matmul", psA[:, 512 * c:512 * (c + 1)],
                         qsb[:, jt, 128 * (b % SLAB):128 * (b % SLAB) + 128],
                         ksb[:, jt, 512 * c:512 * (c + 1)],
                         start=(jt == 0), stop=(jt == 3),
                         sem="pe" if jt == 3 else None)
            return p.counts["pe"]

        def emit_softmax(b, pe_sc):
            g = b // 4
            nch = g + 1
            i = b % 2
            p.wait("vector", "pe", pe_sc)
            p.op("vector", "tensor_add", psA[:, 512 * g:512 * (g + 1)],
                 psA[:, 512 * g:512 * (g + 1)], masks[:, b % 4, :], sem="ve")
            p.wait("vector", "ve")
            p.op("vector", "reduce_max", negmx[i][:, :], psA[:, 0:512 * nch],
                 axis=AX.X, negate=True, sem="ve")
            ve_mx = p.counts["ve"]
            p.wait("scalar", "ve", ve_mx)
            if b - 2 in pe_pt_end:
                p.wait("scalar", "pe", pe_pt_end[b - 2])
            p.op("scalar", "activation", psb[i][:, 0:512 * nch], psA[:, 0:512 * nch],
                 AF.Exp, bias=negmx[i][:, :], scale=1.0, accum_out=den[i][:, :], sem="se")
            se_exp[b] = p.counts["se"]
            p.wait("vector", "se", se_exp[b])
            p.op("vector", "reciprocal", zsc[i][:, :], den[i][:, :], sem="ve")
            p.wait("vector", "ve")
            p.op("vector", "tensor_scalar_mul", zsc[i][:, :], zsc[i][:, :],
                 1.0 / math.sqrt(128.0), sem="ve")
            ve_zs[b] = p.counts["ve"]

        def emit_tail(b):
            nst = b + 1
            i = b % 2
            ysem = "ya" if i == 0 else "yb"
            p.wait("tensor", "se", se_exp[b])
            for sp in range(0, nst, 2):
                w = min(2, nst - sp)
                tb = (sp // 2) % 2
                if evac.get(tb):
                    p.wait("tensor", evac[tb][0], evac[tb][1])
                for h in range(w):
                    st = sp + h
                    p.op("tensor", "matmul", psTA[tb][:, 128 * h:128 * (h + 1)],
                         psb[i][:, 128 * st:128 * (st + 1)], identb[:, :],
                         is_transpose=True, start=(h == 0), stop=(h == w - 1),
                         sem="pe" if h == w - 1 else None)
                pe_t = p.counts["pe"]
                p.wait("scalar", "pe", pe_t)
                p.op("scalar", "copy", ptsb[:, sp:sp + w, :],
                     psTA[tb][:, 0:128 * w], sem="se")
                evac[tb] = ("se", p.counts["se"])
            pe_pt_end[b] = p.counts["pe"]
            se_pt = p.counts["se"]

            p.wait("tensor", "se", se_pt)
            if b - 1 in ve_zlast:
                p.wait("tensor", "ve", ve_zlast[b - 1])
            for st in range(nst):
                p.op("tensor", "matmul", psZ[:, :], ptsb[:, st, :], vsb[:, st, :],
                     start=(st == 0), stop=(st == nst - 1),
                     sem="pe" if st == nst - 1 else None)
            pe_z = p.counts["pe"]
            p.wait("scalar", "pe", pe_z)
            p.wait("scalar", "ve", ve_zs[b])
            p.op("scalar", "activation", zfb[:, :], psZ[:, :], AF.Identity,
                 scale=zsc[i][:, :], sem="se")
            se_zf = p.counts["se"]

            p.wait("tensor", "se", se_zf)
            for jp in range(2):
                tb = jp % 2
                if evac.get(tb):
                    p.wait("tensor", evac[tb][0], evac[tb][1])
                for h in range(2):
                    jt = 2 * jp + h
                    p.op("tensor", "matmul", psTA[tb][:, 128 * h:128 * (h + 1)],
                         zfb[:, 128 * jt:128 * (jt + 1)], identb[:, :],
                         is_transpose=True, start=(h == 0), stop=(h == 1),
                         sem="pe" if h == 1 else None)
                pe_t = p.counts["pe"]
                p.wait("scalar", "pe", pe_t)
                p.op("scalar", "copy", zftsb[:, 2 * jp:2 * jp + 2, :],
                     psTA[tb][:, :], sem="se")
                evac[tb] = ("se", p.counts["se"])
            ve_zt = p.counts["se"]

            p.wait("tensor", "se", ve_zt)
            if b - 2 in st_done:
                p.wait("vector", st_done[b - 2][0], st_done[b - 2][1])
            p.wait("vector", ysem)
            if b - 1 in ve_plast:
                p.wait("tensor", "ve", ve_plast[b - 1])
            for e in range(4):
                dst = psP if e % 2 == 0 else psZ
                if e >= 2:
                    p.wait("tensor", "ve", ve_x1c[e - 2])
                for jt in range(4):
                    p.op("tensor", "matmul", dst[:, :],
                         zftsb[:, jt, :], proj4[:, jt, 512 * e:512 * (e + 1)],
                         start=(jt == 0), stop=(jt == 3),
                         sem="pe" if jt == 3 else None)
                pe_e = p.counts["pe"]
                p.wait("vector", "pe", pe_e)
                p.op("vector", "tensor_add", x1b[i][:, 512 * e:512 * (e + 1)],
                     x2b[i][:, 512 * e:512 * (e + 1)], dst[:, :], sem="ve")
                ve_x1c[e] = p.counts["ve"]
            ve_zlast[b] = ve_x1c[3]
            ve_plast[b] = ve_x1c[2]
            x1_done[b] = ve_x1c[3]
            p.wait("sync", "ve", x1_done[b])
            ssem = "sa" if i == 0 else "sb"
            p.dma("sync", out_h[128 * b:128 * (b + 1), :], x1b[i][:, :], sem=ssem)
            st_done[b] = (ssem, p.counts[ssem])

        ve_x1c = {}
        def ln_stage(j):
            b_tr = j + SLAB
            if 0 <= b_tr < NT:
                emit_norm(b_tr)
                emit_trans(b_tr)
            b_ld = j + SLAB + 2
            if 0 <= b_ld < NT:
                emit_load(b_ld)
            b_st = j + SLAB + 1
            if 0 <= b_st < NT:
                emit_stats(b_st)

        ci = [0]

        def emit_cast_chunk(j):
            if ci[0] >= len(cast_chunks):
                return
            b_ld = j + SLAB + 2
            if 0 <= b_ld < NT:
                p.wait("gpsimd", "xa" if b_ld % 2 == 0 else "xb", ld_cnt[b_ld])
            g, (dst, src) = cast_chunks[ci[0]]
            p.dma("gpsimd", dst, src, sem="gd")
            ci[0] += 1
            if ci[0] >= len(cast_chunks) or cast_chunks[ci[0]][0] != g:
                p.wait("gpsimd", "gd")
                cast_cnt[g] = p.counts["gd"]

        for j in range(-(SLAB + 2), NT):
            if j >= 0:
                pe_sc = emit_scores_pe(j)
                emit_softmax(j, pe_sc)
            ln_stage(j)
            if j >= 0:
                emit_cast_chunk(j)
                if j >= NT - 3:
                    while ci[0] < len(cast_chunks):
                        emit_cast_chunk(j)
            if j >= 1:
                emit_tail(j - 1)
            if j == -1:
                emit_qkv(0, None)
            elif j >= 0 and (j + 1) % SLAB == 0 and 1 <= (j + 1) // SLAB < NSLAB:
                emit_qkv((j + 1) // SLAB, ("se", se_exp[j]))
        emit_tail(NT - 1)

        p.wait("sync", "sa")
        p.wait("sync", "sb")

    p.run_block(b1)
    esB1.close()
    esA.close()

    esH = contextlib.ExitStack()
    EH = esH.enter_context
    h2T = EH(nc.sbuf_tensor("h2T", [128, ND, THALF], FP8))
    L2 = mk_ln_bufs(EH, tag="f_")
    act = EH(nc.sbuf_tensor("act", [128, GH, THALF], FP8))
    wgq = [[EH(nc.sbuf_tensor(f"wgq{gi}_{i}", [128, ND, 512], FP8))
            for i in range(2)] for gi in range(2)]
    wlsb = [EH(nc.sbuf_tensor(f"wlsb{i}", [128, GH, 1024], FP8)) for i in range(2)]
    gel = EH(nc.sbuf_tensor("gel", [128, THALF], BF16))
    stg = [EH(nc.sbuf_tensor(f"stg{i}", [128, 1024], F32)) for i in range(2)]
    psF0 = EH(nc.psum_tensor("psF0", [128, 2048], F32))
    psF1 = EH(nc.psum_tensor("psF1", [128, 2048], F32))

    NCH = max(1, THALF // 512)
    CW = min(512, THALF)

    def ffn(p):
        f0_free = {}
        f1_free = {}
        ve_flush = {}
        ac_cnt = {}
        wg_loaded = {}
        wl_loaded = {}
        st = dict(pe_accum_end=0, prev_norm={}, prev_tr={}, prev_act={})

        def emit_trans_f32(b, ve_ready, rot):
            p.wait("tensor", "ve", ve_ready)
            for kp in range(ND // 2):
                r = (rot + kp) % 4
                if r in f1_free:
                    p.wait("tensor", f1_free[r][0], f1_free[r][1])
                for h in range(2):
                    k = 2 * kp + h
                    p.op("tensor", "matmul", psF1[:, 512 * r + 128 * h:512 * r + 128 * (h + 1)],
                         L2["hb"][b % 2][:, 128 * k:128 * (k + 1)], identb[:, :],
                         start=(h == 0), stop=(h == 1),
                         sem="pe" if h == 1 else None)
                pe_t = p.counts["pe"]
                p.wait("vector", "pe", pe_t)
                p.op("vector", "tensor_copy",
                     h2T[:, 2 * kp:2 * kp + 2, 128 * (b % MH):128 * (b % MH) + 128],
                     psF1[:, 512 * r:512 * r + 256], sem="ve")
                f1_free[r] = ("ve", p.counts["ve"])
            return p.counts["pe"]

        def b2_stages(hf):
            pe_tr2 = dict(st["prev_tr"])
            ve_n2 = dict(st["prev_norm"])
            ld2 = {}
            for j in range(-4, MH - 1):
                mm = j + 2
                if 0 <= mm < MH:
                    hbf = ("pe", pe_tr2[mm - 2]) if mm - 2 in pe_tr2 else None
                    ve_n2[mm] = emit_ln_norm(p, L2, mm, hb_free=hbf)
                    pe_tr2[mm] = emit_trans_f32(mm, ve_n2[mm], rot=2 * mm)
                ml = j + 4
                if 0 <= ml < MH:
                    tb = MH * hf + ml
                    ld2[ml] = emit_ln_load(p, L2, ml, out_h[128 * tb:128 * (tb + 1), :],
                                           buf_free=ve_n2.get(ml - 2))
                ms = j + 3
                if 0 <= ms < MH:
                    emit_ln_stats(p, L2, ms, ld2[ms])
            st["prev_tr"] = {-2: pe_tr2[MH - 2], -1: pe_tr2[MH - 1]}
            st["prev_norm"] = {-2: ve_n2[MH - 2], -1: ve_n2[MH - 1]}
            st["h2T_ready"] = p.counts["ve"]

        def load_wg_quarter(qi, buf, first=False):
            wsem = "wga" if buf == 0 else "wgb"
            col = 512 * qi
            if first:
                p.wait("sync", "pe", p.counts["pe"])
            p.wait("sync", "gd", cast_cnt[qi * 512 // (GH * 128)])
            for gi in range(2):
                for kk in range(ND // 4):
                    src = wgb_h[gi, 512 * kk:512 * (kk + 1), col:col + 512]
                    src = src.rearrange("(k p) c -> p k c", p=128)
                    p.dma("sync", wgq[gi][buf][:, 4 * kk:4 * (kk + 1), :],
                          src, sem=wsem)
            wg_loaded[buf] = p.counts[wsem]
            return wsem

        def load_wl(g, eh, buf):
            wsem = "wla" if buf == 0 else "wlb"
            p.wait("scalar", "gd", cast_cnt[g])
            if st["pe_accum_end"]:
                p.wait("scalar", "pe", st["pe_accum_end"])
            for kk in range(GH // 4):
                r = 128 * GH * g + 512 * kk
                src = wlb_h[r:r + 512, 1024 * eh:1024 * (eh + 1)]
                src = src.rearrange("(k p) c -> p k c", p=128)
                p.dma("scalar", wlsb[buf][:, 4 * kk:4 * (kk + 1), :],
                      src, sem=wsem)
            wl_loaded[buf] = p.counts[wsem]
            return wsem

        def b3_stages(hf):
            ve_act = dict(st["prev_act"])
            se_gel = {}
            p.wait("tensor", "ve", st["h2T_ready"])
            load_wg_quarter(0, 0, first=True)
            wl_sems = {}
            for g in range(NG):
                wl_sems[0] = load_wl(g, 0, 0)
                wl_sems[1] = load_wl(g, 1, 1)
                nq = max(1, GH // 4)
                hq_per = min(4, GH)
                for q in range(nq):
                    buf = q % 2
                    qi = nq * g + q
                    wsem = "wga" if buf == 0 else "wgb"
                    p.wait("tensor", wsem, wg_loaded[buf])
                    if qi + 1 < nq * NG:
                        p.wait("sync", "pe", p.counts["pe"])
                        load_wg_quarter(qi + 1, 1 - buf)
                    for hq in range(hq_per):
                        gh = hq_per * q + hq
                        par = gh % 2
                        for n in range(NCH):
                            r = 2 * par + n
                            nsl = slice(CW * n, CW * (n + 1))
                            fsl = slice(512 * r, 512 * r + CW)
                            if r in f0_free:
                                p.wait("tensor", f0_free[r][0], f0_free[r][1])
                            ND2 = ND // 2
                            for k in range(ND2):
                                p.op("tensor", "matmul", psF0[:, fsl],
                                     wgq[0][buf][:, 2 * k:2 * k + 2, 128 * hq:128 * (hq + 1)],
                                     h2T[:, 2 * k:2 * k + 2, nsl],
                                     start=(k == 0), stop=(k == ND2 - 1),
                                     perf_mode=PM.DoubleRow,
                                     sem="pe" if k == ND2 - 1 else None)
                            pe_g0 = p.counts["pe"]
                            if r in f1_free:
                                p.wait("tensor", f1_free[r][0], f1_free[r][1])
                            for k in range(ND2):
                                p.op("tensor", "matmul", psF1[:, fsl],
                                     wgq[1][buf][:, 2 * k:2 * k + 2, 128 * hq:128 * (hq + 1)],
                                     h2T[:, 2 * k:2 * k + 2, nsl],
                                     start=(k == 0), stop=(k == ND2 - 1),
                                     perf_mode=PM.DoubleRow,
                                     sem="pe" if k == ND2 - 1 else None)
                            pe_g1 = p.counts["pe"]
                            p.wait("scalar", "pe", pe_g0)
                            if (gh - 1, n) in ve_act:
                                p.wait("scalar", "ve", ve_act[(gh - 1, n)])
                            p.op("scalar", "activation", gel[:, nsl], psF0[:, fsl],
                                 AF.Gelu_apprx_tanh, sem="se")
                            se_gel[(gh, n)] = p.counts["se"]
                            f0_free[r] = ("se", se_gel[(gh, n)])
                            p.wait("vector", "pe", pe_g1)
                            p.wait("vector", "se", se_gel[(gh, n)])
                            if st["pe_accum_end"]:
                                p.wait("vector", "pe", st["pe_accum_end"])
                            p.op("vector", "tensor_mul", act[:, gh, nsl],
                                 gel[:, nsl], psF1[:, fsl], sem="ve")
                            ve_act[(gh, n)] = p.counts["ve"]
                            f1_free[r] = ("ve", ve_act[(gh, n)])
                ve_act_all = p.counts["ve"]

                p.wait("tensor", "ve", ve_act_all)
                for eh in range(2):
                    p.wait("tensor", wl_sems[eh], wl_loaded[eh])
                    for m in range(MH):
                        ob = m % 2
                        psO = psF0[:, 1024 * ob:1024 * (ob + 1)]
                        for n in range(2):
                            r = 2 * ob + n
                            if r in f0_free:
                                p.wait("tensor", f0_free[r][0], f0_free[r][1])
                        GH2 = GH // 2
                        for h in range(GH2):
                            for e2 in range(2):
                                p.op("tensor", "matmul", psO[:, 512 * e2:512 * (e2 + 1)],
                                     act[:, 2 * h:2 * h + 2, 128 * m:128 * (m + 1)],
                                     wlsb[eh][:, 2 * h:2 * h + 2, 512 * e2:512 * (e2 + 1)],
                                     start=(h == 0), stop=(h == GH2 - 1),
                                     perf_mode=PM.DoubleRow,
                                     sem="pe" if (h == GH2 - 1 and e2 == 1) else None)
                        pe_o = p.counts["pe"]
                        p.wait("vector", "pe", pe_o)
                        asem = "aca" if ob == 0 else "acb"
                        if ob in ac_cnt:
                            p.wait("vector", asem, ac_cnt[ob])
                        p.op("vector", "tensor_copy", stg[ob][:, :], psO[:, :], sem="ve")
                        ve_flush[ob] = p.counts["ve"]
                        for n in range(2):
                            f0_free[2 * ob + n] = ("ve", ve_flush[ob])
                        p.wait("gpsimd", "ve", ve_flush[ob])
                        r0 = 128 * (MH * hf + m)
                        p.dma("gpsimd", out_h[r0:r0 + 128, 1024 * eh:1024 * (eh + 1)],
                              stg[ob][:, :], sem=asem, accum_op=ALU.add)
                        ac_cnt[ob] = p.counts[asem]
                st["pe_accum_end"] = p.counts["pe"]
                st["prev_act"] = {(-1, n): ve_act_all for n in range(NCH)}
                ve_act = dict(st["prev_act"])
                se_gel = {}

        for hf in range(NHALF):
            b2_stages(hf)
            b3_stages(hf)
        p.wait("gpsimd", "aca")
        p.wait("gpsimd", "acb")

    p.run_block(ffn)
    esH.close()
    es.close()
    p._es.close()
    return nc


def kernel(L0_attn_x, L0_attn_WQ, L0_attn_bQ, L0_attn_WK, L0_attn_bK,
           L0_attn_WV, L0_attn_bV, L0_attn_proj, L0_ffn_w_gating, L0_ffn_w_linear):
    x = np.ascontiguousarray(np.asarray(L0_attn_x, dtype=np.float32))
    W = dict(
        WQ=np.ascontiguousarray(np.asarray(L0_attn_WQ, dtype=np.float32)),
        bQ=np.ascontiguousarray(np.asarray(L0_attn_bQ, dtype=np.float32)),
        WK=np.ascontiguousarray(np.asarray(L0_attn_WK, dtype=np.float32)),
        bK=np.ascontiguousarray(np.asarray(L0_attn_bK, dtype=np.float32)),
        WV=np.ascontiguousarray(np.asarray(L0_attn_WV, dtype=np.float32)),
        bV=np.ascontiguousarray(np.asarray(L0_attn_bV, dtype=np.float32)),
        proj=np.ascontiguousarray(np.asarray(L0_attn_proj, dtype=np.float32)),
        wg=np.ascontiguousarray(np.asarray(L0_ffn_w_gating, dtype=np.float32)),
        wl=np.ascontiguousarray(np.asarray(L0_ffn_w_linear, dtype=np.float32)),
    )
    nc = build_nc(T=2048, D=2048, H=8192)
    in_maps = [dict(x=x[c], **W) for c in range(8)]
    res = run_bass_kernel_spmd(nc, in_maps, core_ids=list(range(8)))
    return np.stack([np.asarray(r["out"], dtype=np.float32) for r in res.results], axis=0)
